# version 05
# baseline (speedup 1.0000x reference)
import os
import numpy as np
import ml_dtypes
import concourse.bass as bass
import concourse.mybir as mybir
from concourse.bass_utils import run_bass_kernel_spmd
from concourse.aot_env import allow_ici_all_to_all

F32 = mybir.dt.float32
BF16 = mybir.dt.bfloat16
I32 = mybir.dt.int32
AF = mybir.ActivationFunctionType
ALU = mybir.AluOpType
AX = mybir.AxisListType

NC, B, L, D = 8, 8, 4096, 1024
NH, HD, ZD, NE, LK, HS = 16, 64, 64, 8, 512, 128
TPC, NT128, NT512 = 4096, 32, 8

MAGIC = float(np.float32(1.5 * 2 ** 23))
INV2PI = float(np.float32(1.0 / (2 * np.pi)))
TWOPI = float(np.float32(2 * np.pi))
PIO2 = float(np.pi / 2)
EPS_LN, EPS_RMS = 1e-5, 1e-6
PI_LO = 3.1415925
B3_HI, B3_LO = 1.5707960, -4.7123880


def build_program(debug=False, phase_limit=99, s_limit=99, conv_mode=4, conv_h=128):
    nc = bass.Bass("TRN2", target_bir_lowering=False)

    x_tok = nc.dram_tensor("x_tok", [TPC, D], F32, kind="ExternalInput")
    x_conv = nc.dram_tensor("x_conv", [HS, 128, B, 32], F32, kind="ExternalInput")
    wq_d = nc.dram_tensor("wq", [D, D], F32, kind="ExternalInput")
    bq_f = nc.dram_tensor("bq_f", [128, 8], F32, kind="ExternalInput")
    wdown_d = nc.dram_tensor("wdown", [D, ZD], F32, kind="ExternalInput")
    bdown_f = nc.dram_tensor("bdown_f", [ZD, 1], F32, kind="ExternalInput")
    wup_d = nc.dram_tensor("wup", [ZD, 2 * D], F32, kind="ExternalInput")
    bup_r = nc.dram_tensor("bup_r", [1, 2 * D], F32, kind="ExternalInput")
    wo_d = nc.dram_tensor("wo", [D, D], F32, kind="ExternalInput")
    bo_r = nc.dram_tensor("bo_r", [1, D], F32, kind="ExternalInput")
    wg_d = nc.dram_tensor("wg", [D, NE], F32, kind="ExternalInput")
    bg_f = nc.dram_tensor("bg_f", [NE, 1], F32, kind="ExternalInput")
    we_d = nc.dram_tensor("we", [NE, D, D], F32, kind="ExternalInput")
    g1_f = nc.dram_tensor("g1_f", [128, 8], F32, kind="ExternalInput")
    g2_f = nc.dram_tensor("g2_f", [128, 8], F32, kind="ExternalInput")
    logdt_r = nc.dram_tensor("logdt_r", [1, HS], F32, kind="ExternalInput")
    lre_r = nc.dram_tensor("lre_r", [1, 64], F32, kind="ExternalInput")
    lim_r = nc.dram_tensor("lim_r", [1, 64], F32, kind="ExternalInput")
    wre_d = nc.dram_tensor("wre_h", [HS, 64], F32, kind="ExternalInput")
    wim_d = nc.dram_tensor("wim_h", [HS, 64], F32, kind="ExternalInput")
    dcol_d = nc.dram_tensor("dcol_h", [HS, 1], F32, kind="ExternalInput")

    out_d = nc.dram_tensor("out", [TPC, D], F32, kind="ExternalOutput")
    dbg = {}
    if debug:
        for nm, shp in [("kbuf", [128, 768]), ("stats", [128, 64]),
                        ("qf", [128, 4096]), ("kf", [128, 4096]),
                        ("aof", [128, 4096]), ("acc", [128, 4096]),
                        ("h2", [128, 4096]), ("gt", [128, 32]),
                        ("ys", [128, 1024]), ("u2n", [128, 288]), ("xmid", [128, 1024]), ("xnd", [128, 1024]),
                        ("e2", [128, 2048]), ("vstk", [128, 1056])]:
            dbg[nm] = nc.dram_tensor("dbg_" + nm, shp, F32, kind="ExternalOutput")

    statsS = nc.dram_tensor("statsS", [128, 64], F32)
    statsAll = nc.dram_tensor("statsAll", [NC * 128, 64], F32)
    agin = nc.dram_tensor("agin", [B * 4096, 128], F32)
    yall = nc.dram_tensor("yall", [NC * 4096, 128], F32)
    we_bf = nc.dram_tensor("we_bf", [NE * D, D], BF16)
    vscr = nc.dram_tensor("vscr", [TPC, D], BF16)

    from contextlib import ExitStack
    stack = ExitStack()
    en = stack.enter_context

    ident_b = en(nc.sbuf_tensor("ident_b", [128, 128], BF16))
    ident_f = en(nc.sbuf_tensor("ident_f", [128, 128], F32))
    mask_b = en(nc.sbuf_tensor("mask_b", [128, 128], BF16))
    iota_p = en(nc.sbuf_tensor("iota_p", [128, 128], I32))
    iota_c = en(nc.sbuf_tensor("iota_c", [128, 128], I32))
    ones_r = en(nc.sbuf_tensor("ones_r", [1, 128], BF16))
    lrow = en(nc.sbuf_tensor("lrow", [128, 64], F32))
    consts = en(nc.sbuf_tensor("consts", [128, 4], F32))
    CEPS_LN, CEPS_RMS, CPIO2 = 0, 1, 2

    st_m = en(nc.sbuf_tensor("st_m", [128, 32], F32))
    st_r = en(nc.sbuf_tensor("st_r", [128, 32], F32))
    st_nmr = en(nc.sbuf_tensor("st_nmr", [128, 32], F32))
    st_s2 = en(nc.sbuf_tensor("st_s2", [128, 32], F32))
    st_nms2 = en(nc.sbuf_tensor("st_nms2", [128, 32], F32))
    st_tmp = en(nc.sbuf_tensor("st_tmp", [128, 32], F32))
    st_sx = en(nc.sbuf_tensor("st_sx", [128, 32], F32))
    st_sxx = en(nc.sbuf_tensor("st_sxx", [128, 32], F32))
    st_r3 = en(nc.sbuf_tensor("st_r3", [128, 32], F32))
    stfold = en(nc.sbuf_tensor("stfold", [128, 64], F32))
    scrap = en(nc.sbuf_tensor("scrap", [128, 1024], F32))

    wq_sb = en(nc.sbuf_tensor("wq_sb", [128, 64 * 128], BF16))
    wdown_sb = en(nc.sbuf_tensor("wdown_sb", [128, 8 * 64], BF16))
    wupk_sb = en(nc.sbuf_tensor("wupk_sb", [65, 8 * 128], BF16))
    wupv_sb = en(nc.sbuf_tensor("wupv_sb", [65, 1024], BF16))
    wo_sb = en(nc.sbuf_tensor("wo_sb", [128, 8 * 1024], BF16))
    wg_sb = en(nc.sbuf_tensor("wg_sb", [128, 8 * 8], BF16))
    bo_sb = en(nc.sbuf_tensor("bo_sb", [1, 1024], BF16))
    bq_sb = en(nc.sbuf_tensor("bq_sb", [128, 8], F32))
    bdown_sb = en(nc.sbuf_tensor("bdown_sb", [64, 1], F32))
    bg_sb = en(nc.sbuf_tensor("bg_sb", [8, 1], F32))
    g1_sb = en(nc.sbuf_tensor("g1_sb", [128, 8], F32))
    g2_sb = en(nc.sbuf_tensor("g2_sb", [128, 8], F32))
    wstage = en(nc.sbuf_tensor("wstage", [128, 2 * 1024], F32))
    h2tmb = en(nc.sbuf_tensor("h2tmb", [128, 1024], BF16))
    we_sb = en(nc.sbuf_tensor("we_sb", [128, 2 * 1024], BF16))

    dt_r = en(nc.sbuf_tensor("dt_r", [1, 128], F32))
    elre_r = en(nc.sbuf_tensor("elre_r", [1, 64], F32))
    lim_sb = en(nc.sbuf_tensor("lim_sb", [1, 64], F32))
    a_sb = en(nc.sbuf_tensor("a_sb", [128, 64], F32))
    b_sb = en(nc.sbuf_tensor("b_sb", [128, 64], F32))
    wre_sb = en(nc.sbuf_tensor("wre_sb", [128, 64], F32))
    wim_sb = en(nc.sbuf_tensor("wim_sb", [128, 64], F32))
    dcol_sb = en(nc.sbuf_tensor("dcol_sb", [128, 1], F32))
    qc_sb = en(nc.sbuf_tensor("qc_sb", [128, 64], F32))
    qs_sb = en(nc.sbuf_tensor("qs_sb", [128, 64], F32))
    kacc = en(nc.sbuf_tensor("kacc", [128, 512], F32))
    band = [en(nc.sbuf_tensor(f"band{i}", [128, 640], BF16)) for i in range(2)]
    bandl = [en(nc.sbuf_tensor(f"bandl{i}", [128, 640], BF16))
             for i in range(2)]
    u2raw = [en(nc.sbuf_tensor(f"u2raw{i}", [128, 256], F32)) for i in range(2)]
    u2np = en(nc.sbuf_tensor("u2np", [128, 288], F32))
    u2n = [en(nc.sbuf_tensor(f"u2n{i}", [128, 288], BF16)) for i in range(2)]
    u2l = [en(nc.sbuf_tensor(f"u2l{i}", [128, 288], BF16)) for i in range(2)]
    m2_sb = en(nc.sbuf_tensor("m2_sb", [128, 256], F32))
    r2_sb = en(nc.sbuf_tensor("r2_sb", [128, 256], F32))
    m2r2_sb = en(nc.sbuf_tensor("m2r2_sb", [128, 256], F32))
    yhi = [en(nc.sbuf_tensor(f"yhi{i}", [128, 256], BF16)) for i in range(2)]
    ylo = [en(nc.sbuf_tensor(f"ylo{i}", [128, 256], BF16)) for i in range(2)]
    ystage = [en(nc.sbuf_tensor(f"ystage{i}", [128, 256], F32)) for i in range(2)]

    x_tm = [en(nc.sbuf_tensor(f"x_tm{i}", [128, 1024], F32)) for i in range(2)]
    xd = [en(nc.sbuf_tensor(f"xd{i}", [128, 1024], F32)) for i in range(2)]
    xnd = en(nc.sbuf_tensor("xnd", [128, 1024], F32))
    xmid = [en(nc.sbuf_tensor(f"xmid{i}", [128, 1024], F32)) for i in range(2)]
    h1tm = [en(nc.sbuf_tensor(f"h1tm{i}", [128, 1024], BF16)) for i in range(2)]
    h1fm = en(nc.sbuf_tensor("h1fm", [128, 8 * 512], BF16))
    h2fm = en(nc.sbuf_tensor("h2fm", [128, 8 * 512], BF16))
    qf = en(nc.sbuf_tensor("qf", [128, 8 * 512], BF16))
    kfb = en(nc.sbuf_tensor("kfb", [128, 8 * 512], BF16))
    zaug = en(nc.sbuf_tensor("zaug", [65, 512], BF16))
    vtm = [en(nc.sbuf_tensor(f"vtm{i}", [128, 1024], BF16)) for i in range(2)]
    kp_sb = en(nc.sbuf_tensor("kp_sb", [128, 1024], BF16))
    qp_sb = en(nc.sbuf_tensor("qp_sb", [128, 1024], BF16))
    e_sb = en(nc.sbuf_tensor("e_sb", [128, 2048], BF16))
    vstk = en(nc.sbuf_tensor("vstk", [128, 16 * 66], BF16))
    avn = en(nc.sbuf_tensor("avn", [128, 1024], BF16))
    recb = en(nc.sbuf_tensor("recb", [128, 16], F32))
    aof = en(nc.sbuf_tensor("aof", [128, 8 * 512], BF16))
    accb = en(nc.sbuf_tensor("accb", [128, 4 * 1024], F32))
    glog = en(nc.sbuf_tensor("glog", [8, 512], F32))
    glogt = en(nc.sbuf_tensor("glogt", [128, 32], F32))
    gtm = en(nc.sbuf_tensor("gtm", [128, 32], F32))
    grec = en(nc.sbuf_tensor("grec", [128, 8], F32))
    ybf = en(nc.sbuf_tensor("ybf", [128, 1024], F32))
    ybhi = en(nc.sbuf_tensor("ybhi", [128, 1024], BF16))
    yblo = en(nc.sbuf_tensor("yblo", [128, 1024], BF16))
    outtm = [en(nc.sbuf_tensor(f"outtm{i}", [128, 1024], F32)) for i in range(2)]

    def kkc(i, a=0, b=512):
        if i < 8:
            return accb[:, i * 512 + a: i * 512 + b]
        return outtm[0][:, (i - 8) * 512 + a: (i - 8) * 512 + b]

    pf = [en(nc.psum_tensor(f"pf{i}", [128, 512], F32)) for i in range(6)]
    pb = [en(nc.psum_tensor(f"pb{i}", [128, 1024], BF16)) for i in range(2)]

    sLD = en(nc.semaphore("sLD"))
    sG = en(nc.semaphore("sG"))
    sPE = en(nc.semaphore("sPE"))
    sVE = en(nc.semaphore("sVE"))
    sSE = en(nc.semaphore("sSE"))
    sC1 = en(nc.semaphore("sC1"))
    sC2 = en(nc.semaphore("sC2"))
    block = en(nc.Block())

    Q = {k: [] for k in ("sync", "tensor", "vector", "scalar", "gpsimd")}
    C = {"LD": 0, "PE": 0, "VE": 0, "SE": 0, "G": 0}

    def emit(eng, fn):
        Q[eng].append(fn)

    def wait(eng, sem, val):
        if val and val > 0:
            Q[eng].append(lambda x, s=sem, v=val: x.wait_ge(s, v))

    DS = {}

    def dsem(name):
        if name not in DS:
            DS[name] = [en(nc.semaphore("s_" + name)), 0]
        return DS[name]

    def ld(dst_ap, src_ap, st="p0"):
        d = dsem(st)
        emit("sync", lambda x, dd=dst_ap, s=src_ap, q=d[0]:
             x.dma_start(out=dd, in_=s).then_inc(q, 16))
        d[1] += 16
        return (d[0], d[1])

    def gld(dst_ap, src_ap, st):
        d = dsem(st)
        emit("gpsimd", lambda g, dd=dst_ap, s=src_ap, q=d[0]:
             g.dma_start(out=dd, in_=s).then_inc(q, 16))
        d[1] += 16
        return (d[0], d[1])

    def waitm(eng, mark):
        sem, val = mark
        wait(eng, sem, val)

    def quiesce(eng="sync"):
        for sem, cnt in DS.values():
            wait(eng, sem, cnt)

    def group(eng, key, sem, fns):
        for f in fns[:-1]:
            emit(eng, f)
        emit(eng, lambda x, f=fns[-1], s=sem: f(x).then_inc(s, 1))
        C[key] += 1
        return C[key]

    def pe_grp(fns):
        return group("tensor", "PE", sPE, fns)

    def ve_grp(fns):
        return group("vector", "VE", sVE, fns)

    def se_grp(fns):
        return group("scalar", "SE", sSE, fns)

    def gp_grp(fns):
        return group("gpsimd", "G", sG, fns)

    def ap3(base, dims, off=0):
        a = base.ap
        if dims:
            a[1] = dims[0]
        for dd in dims[1:]:
            a.append(dd)
        base.offset = base.offset + off
        return base

    LD_params = 0
    for dst, src in [
        (bq_sb[:, :], bq_f[:, :]), (bdown_sb[:, :], bdown_f[:, :]),
        (bg_sb[:, :], bg_f[:, :]), (g1_sb[:, :], g1_f[:, :]),
        (g2_sb[:, :], g2_f[:, :]), (dt_r[:, :], logdt_r[:, :]),
        (lim_sb[:, :], lim_r[:, :]), (elre_r[:, :], lre_r[:, :]),
        (wre_sb[:, :], wre_d[:, :]), (wim_sb[:, :], wim_d[:, :]),
        (dcol_sb[:, :], dcol_d[:, :]),
    ]:
        LD_params = ld(dst, src)

    G_iotas = gp_grp(
        [lambda g: g.iota(iota_c[:, :], pattern=[[1, 128]], base=0,
                          channel_multiplier=0),
         lambda g: g.iota(iota_p[:, :], pattern=[[0, 128]], base=0,
                          channel_multiplier=1),
         lambda g: g.iota(lrow[:, :], pattern=[[1, 64]], base=0,
                          channel_multiplier=0,
                          allow_small_or_imprecise_dtypes=True)]
        + [lambda g, t=t_, v=v_: g.memset(t, v) for (t_, v_) in
           [(ones_r[:, :], 1.0), (consts[:, 0:1], EPS_LN),
            (consts[:, 1:2], EPS_RMS), (consts[:, 2:3], PIO2),
            (zaug[64:65, :], 1.0), (vstk[:, :], 1.0), (kacc[:, :], 0.0)]])

    wait("vector", sG, G_iotas)
    VE_ident = ve_grp([
        lambda v: v.tensor_tensor(ident_b[:, :], iota_p[:, :], iota_c[:, :],
                                  ALU.is_equal),
        lambda v: v.tensor_tensor(ident_f[:, :], iota_p[:, :], iota_c[:, :],
                                  ALU.is_equal)])
    wait("gpsimd", sVE, VE_ident)
    G_mi = gp_grp([lambda g: g.iota(iota_p[:, :], pattern=[[-16, 8], [0, 16]],
                                    base=0, channel_multiplier=1)])
    wait("vector", sG, G_mi)
    VE_mask_a = ve_grp([
        lambda v: v.tensor_scalar(scrap[:, 0:128], iota_p[:, :], 0, None,
                                  ALU.is_ge, ALU.bypass),
        lambda v: v.tensor_scalar(scrap[:, 128:256], iota_p[:, :], 15, None,
                                  ALU.is_le, ALU.bypass)])
    wait("vector", sVE, VE_mask_a)
    VE_mask = ve_grp([
        lambda v: v.tensor_tensor(mask_b[:, :], scrap[:, 0:128],
                                  scrap[:, 128:256], ALU.mult)])

    x_rows = x_tok[:, :].rearrange("(j p) d -> j p d", p=128)
    LD_xs = []
    for j in range(NT128):
        if j >= 2:
            wait("sync", sSE, 2 * (j - 1))
        LD_xs.append(ld(x_tm[j % 2][:, :], x_rows[j, :, :], f"xa{j % 2}"))
    wait("scalar", sVE, VE_mask)
    for j in range(NT128):
        waitm("scalar", LD_xs[j])
        mk_se = se_grp([lambda s, jj=j: s.activation(
            scrap[:, :], x_tm[jj % 2][:, :], AF.Copy,
            accum_out=st_sx[:, jj:jj + 1])])
        wait("scalar", sSE, mk_se)
        mk_se = se_grp([lambda s, jj=j: s.activation(
            scrap[:, :], x_tm[jj % 2][:, :], AF.Square,
            accum_out=st_sxx[:, jj:jj + 1])])
        wait("scalar", sSE, mk_se)
    SE_stats = C["SE"]

    wait("vector", sSE, SE_stats)
    VE_m0 = ve_grp([lambda v: v.tensor_scalar(
        st_m[:, :], st_sx[:, :], 1.0 / D, None, ALU.mult, ALU.bypass)])
    wait("vector", sVE, VE_m0)
    VE_q0 = ve_grp([lambda v, jj=j: v.tensor_scalar(
        st_tmp[:, jj:jj + 1], st_m[:, jj:jj + 1], st_m[:, jj:jj + 1],
        None, ALU.mult, ALU.bypass) for j in range(32)])
    wait("vector", sVE, VE_q0)
    VE_var = ve_grp([lambda v, jj=j: v.tensor_scalar(
        st_sx[:, jj:jj + 1], st_sxx[:, jj:jj + 1], 1.0 / D,
        st_tmp[:, jj:jj + 1], ALU.mult, ALU.subtract) for j in range(32)])
    wait("scalar", sVE, VE_var)
    SE_r0 = se_grp([lambda s: s.activation(st_r[:, :], st_sx[:, :], AF.Ln,
                                           bias=consts[:, 0:1])])
    wait("scalar", sSE, SE_r0)
    SE_r = se_grp([lambda s: s.activation(st_r[:, :], st_r[:, :], AF.Exp,
                                          scale=-0.5)])
    wait("vector", sSE, SE_r)
    VE_fold = ve_grp(
        [lambda v, jj=j: v.tensor_scalar(
            st_tmp[:, jj:jj + 1], st_sx[:, jj:jj + 1], st_r[:, jj:jj + 1],
            st_r[:, jj:jj + 1], ALU.mult, ALU.mult) for j in range(32)]
        + [lambda v: v.tensor_copy(stfold[:, 0:32], st_m[:, :]),
           lambda v: v.tensor_copy(stfold[:, 32:64], st_r[:, :])])
    wait("scalar", sVE, VE_fold)
    SE_rs0 = se_grp([lambda s: s.activation(st_tmp[:, :], st_tmp[:, :], AF.Ln,
                                            bias=consts[:, 1:2])])
    wait("scalar", sSE, SE_rs0)
    SE_rstd2 = se_grp([lambda s: s.activation(st_tmp[:, :], st_tmp[:, :],
                                              AF.Exp, scale=-0.5)])
    wait("vector", sSE, SE_rstd2)
    VE_minis_a = ve_grp(
        [lambda v, jj=j: v.tensor_scalar(
            st_s2[:, jj:jj + 1], st_r[:, jj:jj + 1], st_tmp[:, jj:jj + 1],
            None, ALU.mult, ALU.bypass) for j in range(32)]
        + [lambda v, jj=j: v.tensor_scalar(
            st_nmr[:, jj:jj + 1], st_m[:, jj:jj + 1], st_r[:, jj:jj + 1],
            -1.0, ALU.mult, ALU.mult) for j in range(32)])
    wait("vector", sVE, VE_minis_a)
    VE_minis = ve_grp(
        [lambda v, jj=j: v.tensor_scalar(
            st_nms2[:, jj:jj + 1], st_m[:, jj:jj + 1], st_s2[:, jj:jj + 1],
            -1.0, ALU.mult, ALU.mult) for j in range(32)])

    wait("sync", sVE, VE_fold)
    LD_statsS = ld(statsS[:, :], stfold[:, :], "sts")
    waitm("gpsimd", LD_statsS)
    emit("gpsimd", lambda g: g.collective_compute(
        "AllGather", ALU.bypass, replica_groups=[list(range(NC))],
        ins=[statsS[:, :].opt()], outs=[statsAll[:, :].opt()]).then_inc(sC1))

    wait("sync", sC1, 1)

    def fold_src(col):
        return ap3(statsAll[0:128, col:col + 1], [(128 * 64, 8), (1, 32)])

    ld(m2_sb[:, :], fold_src(0), "m2r")
    LD_m2 = ld(r2_sb[:, :], fold_src(32), "m2r")
    waitm("vector", LD_m2)
    VE_m2r2 = ve_grp([lambda v: v.tensor_tensor(
        m2r2_sb[:, :], m2_sb[:, :], r2_sb[:, :], ALU.mult)])
    waitm("vector", LD_params)
    VE_bq8 = ve_grp([lambda v: v.tensor_scalar(
        bq_sb[:, :], bq_sb[:, :], 0.125, None, ALU.mult, ALU.bypass)])

    wq_rows = wq_d[:, :].rearrange("(kc p) d -> kc p d", p=128)
    wo_rows = wo_d[:, :].rearrange("(kc p) d -> kc p d", p=128)
    wdn_rows = wdown_d[:, :].rearrange("(kc p) d -> kc p d", p=128)
    wg_rows = wg_d[:, :].rearrange("(kc p) d -> kc p d", p=128)
    we_rows = we_d[:, :, :].rearrange("e (kc p) d -> (e kc) p d", p=128)
    webf_rows = we_bf[:, :].rearrange("(ekc p) d -> ekc p d", p=128)

    cast_jobs = []

    def wq_cast(kc):
        def c_(v, sl):
            return v.tensor_scalar(wq_sb[:, kc * 1024:(kc + 1) * 1024], sl,
                                   g1_sb[:, kc:kc + 1], 0.125,
                                   ALU.mult, ALU.mult)
        return c_

    for kc in range(8):
        cast_jobs.append((wq_rows[kc, :, :], 128, 1024, wq_cast(kc)))
    for kc in range(8):
        cast_jobs.append((wo_rows[kc, :, :], 128, 1024,
                          lambda v, sl, kcc=kc: v.tensor_copy(
                              wo_sb[:, kcc * 1024:(kcc + 1) * 1024], sl)))
    for kc in range(8):
        cast_jobs.append((wdn_rows[kc, :, :], 128, 64,
                          lambda v, sl, kcc=kc: v.tensor_scalar(
                              wdown_sb[:, kcc * 64:(kcc + 1) * 64], sl,
                              g1_sb[:, kcc:kcc + 1], None, ALU.mult, ALU.bypass)))
    for kc in range(8):
        cast_jobs.append((wg_rows[kc, :, :], 128, 8,
                          lambda v, sl, kcc=kc: v.tensor_scalar(
                              wg_sb[:, kcc * 8:(kcc + 1) * 8], sl,
                              g2_sb[:, kcc:kcc + 1], None, ALU.mult, ALU.bypass)))
    cast_jobs.append((bo_r[0:1, :], 1, 1024,
                      lambda v, sl: v.tensor_copy(bo_sb[0:1, :], sl)))

    VE_cd = []
    for i, (src_ap, rows, ncols, cast_emit) in enumerate(cast_jobs):
        half = i % 2
        if i >= 2:
            wait("sync", sVE, VE_cd[i - 2])
        LDi = ld(wstage[0:rows, half * 1024: half * 1024 + ncols], src_ap,
                 f"ws{half}")
        waitm("vector", LDi)
        VE_cd.append(ve_grp([lambda v, ce=cast_emit, r=rows, h=half, ncol=ncols:
                             ce(v, wstage[0:r, h * 1024: h * 1024 + ncol])]))

    wait("sync", sVE, VE_cd[-2])
    wait("sync", sVE, VE_cd[-1])
    ld(wstage[0:64, 0:1024], wup_d[0:64, 0:1024], "wup")
    ld(wstage[0:64, 1024:2048], wup_d[0:64, 1024:2048], "wup")
    ld(wstage[64:65, 0:1024], bup_r[0:1, 0:1024], "wup")
    LD_wup = ld(wstage[64:65, 1024:2048], bup_r[0:1, 1024:2048], "wup")
    waitm("vector", LD_wup)

    def cast_wupk(v):
        dst = ap3(wupk_sb[0:65, 0:1], [(128, 8), (64, 2), (1, 64)])
        src = ap3(wstage[0:65, 0:1], [(256, 8), (128, 2), (1, 64)])
        return v.tensor_copy(dst, src)

    def cast_wupv(v):
        dst = ap3(wupv_sb[0:65, 0:1], [(512, 2), (64, 8), (1, 64)])
        src = ap3(wstage[0:65, 64:65], [(128, 2), (256, 8), (1, 64)])
        return v.tensor_copy(dst, src)

    VE_wup = ve_grp([cast_wupk, lambda v: cast_wupv(v)])

    VE_wem = []
    WE_ST = []
    for i in range(64):
        kc = i % 8
        half = i % 2
        if i <= 1:
            wait("sync", sVE, VE_wup)
        if i >= 2:
            wait("sync", sVE, VE_wem[i - 2])
        LDi = ld(wstage[:, half * 1024:half * 1024 + 1024], we_rows[i, :, :],
                 f"ws{half}")
        waitm("vector", LDi)
        if i >= 1:
            waitm("vector", WE_ST[i - 1])
        VE_wem.append(ve_grp([lambda v, kcc=kc, h=half: v.tensor_scalar(
            h2tmb[:, :], wstage[:, h * 1024:h * 1024 + 1024],
            g2_sb[:, kcc:kcc + 1], None, ALU.mult, ALU.bypass)]))
        wait("sync", sVE, VE_wem[-1])
        WE_ST.append(ld(webf_rows[i, :, :], h2tmb[:, :], "wb"))
    LD_westg_last = WE_ST[-1]

    if phase_limit < 1:
        quiesce()
        return _finish(block, Q, stack, nc)
    waitm("scalar", LD_params)
    SE_dt = se_grp([
        lambda s: s.activation(dt_r[:, :], dt_r[:, :], AF.Exp),
        lambda s: s.activation(elre_r[:, :], elre_r[:, :], AF.Exp)])
    wait("tensor", sSE, SE_dt)
    PE_ab = pe_grp([
        lambda t: t.matmul(pf[2][:, 0:64], dt_r[0:1, :], lim_sb[0:1, :],
                           start=True, stop=True),
        lambda t: t.matmul(pf[2][:, 64:128], dt_r[0:1, :], elre_r[0:1, :],
                           start=True, stop=True)])
    wait("vector", sPE, PE_ab)
    VE_ab = ve_grp([
        lambda v: v.tensor_copy(b_sb[:, :], pf[2][:, 0:64]),
        lambda v: v.tensor_scalar(a_sb[:, :], pf[2][:, 64:128], -1.0, None,
                                  ALU.mult, ALU.bypass)])

    CH = 64
    wait("scalar", sVE, VE_ab)
    SE_qm = se_grp([
        lambda s: s.activation(kkc(0, 0, 64), a_sb[:, :], AF.Exp,
                               scale=float(CH)),
        lambda s: s.activation(scrap[:, 0:64], b_sb[:, :], AF.Copy,
                               scale=float(CH))])
    wait("vector", sSE, SE_qm)
    qsteps = [
        lambda v: v.tensor_scalar(kkc(1, 0, 64), scrap[:, 0:64], INV2PI,
                                  MAGIC, ALU.mult, ALU.add),
        lambda v: v.tensor_scalar(kkc(1, 0, 64), kkc(1, 0, 64), 1.0,
                                  -MAGIC, ALU.mult, ALU.add),
        lambda v: v.scalar_tensor_tensor(kkc(2, 0, 64), kkc(1, 0, 64),
                                         -TWOPI, scrap[:, 0:64],
                                         ALU.mult, ALU.add),
        lambda v: v.tensor_scalar(kkc(1, 0, 64), scrap[:, 0:64], INV2PI,
                                  0.25, ALU.mult, ALU.add),
        lambda v: v.tensor_scalar(kkc(1, 0, 64), kkc(1, 0, 64), 1.0,
                                  MAGIC, ALU.mult, ALU.add),
        lambda v: v.tensor_scalar(kkc(1, 0, 64), kkc(1, 0, 64), 1.0,
                                  -MAGIC, ALU.mult, ALU.add),
        lambda v: v.scalar_tensor_tensor(kkc(3, 0, 64), kkc(1, 0, 64),
                                         -TWOPI, scrap[:, 0:64],
                                         ALU.mult, ALU.add),
        lambda v: v.tensor_scalar(kkc(2, 0, 64), kkc(2, 0, 64), PI_LO,
                                  -PI_LO, ALU.min, ALU.max),
        lambda v: v.tensor_scalar(kkc(3, 0, 64), kkc(3, 0, 64), B3_HI,
                                  B3_LO, ALU.min, ALU.max)]
    for stp in qsteps:
        mkq = ve_grp([stp])
        wait("vector", sVE, mkq)
    VE_qph = C["VE"]
    wait("scalar", sVE, VE_qph)
    SE_qtrig = se_grp([
        lambda s: s.activation(kkc(2, 0, 64), kkc(2, 0, 64), AF.Sin),
        lambda s: s.activation(kkc(3, 0, 64), kkc(3, 0, 64), AF.Sin,
                               bias=consts[:, 2:3])])
    wait("vector", sSE, SE_qtrig)
    VE_q = ve_grp([
        lambda v: v.tensor_tensor(qs_sb[:, :], kkc(0, 0, 64),
                                  kkc(2, 0, 64), ALU.mult),
        lambda v: v.tensor_tensor(qc_sb[:, :], kkc(0, 0, 64),
                                  kkc(3, 0, 64), ALU.mult)])

    def bcast_n(tile, ngc):
        return ap3(tile[0:128, ngc * 8:ngc * 8 + 1], [(1, 8), (0, 64)])

    def rep_l():
        return ap3(lrow[0:128, 0:1], [(0, 8), (1, 64)])

    _vlast = [None]

    def ve_seq(groups):
        for g in groups:
            if _vlast[0] is not None:
                wait("vector", sVE, _vlast[0])
            _vlast[0] = ve_grp(g)
        return _vlast[0]

    wait("vector", sVE, VE_q)
    for ng in range(8):
        mk = ve_seq([
            [lambda v, n=ng: v.tensor_tensor(kkc(0), bcast_n(b_sb, n),
                                             rep_l(), ALU.mult),
             lambda v, n=ng: v.tensor_tensor(kkc(1), bcast_n(a_sb, n),
                                             rep_l(), ALU.mult)],
            [lambda v: v.tensor_scalar(kkc(8), kkc(0), INV2PI,
                                       MAGIC, ALU.mult, ALU.add),
             lambda v: v.tensor_scalar(kkc(9), kkc(0), INV2PI,
                                       0.25, ALU.mult, ALU.add)],
            [lambda v: v.tensor_scalar(kkc(8), kkc(8), 1.0, -MAGIC,
                                       ALU.mult, ALU.add),
             lambda v: v.tensor_scalar(kkc(9), kkc(9), 1.0, MAGIC,
                                       ALU.mult, ALU.add)],
            [lambda v: v.scalar_tensor_tensor(kkc(2), kkc(8),
                                              -TWOPI, kkc(0),
                                              ALU.mult, ALU.add),
             lambda v: v.tensor_scalar(kkc(9), kkc(9), 1.0, -MAGIC,
                                       ALU.mult, ALU.add)],
            [lambda v: v.scalar_tensor_tensor(kkc(3), kkc(9),
                                              -TWOPI, kkc(0),
                                              ALU.mult, ALU.add),
             lambda v: v.tensor_scalar(kkc(2), kkc(2), PI_LO,
                                       -PI_LO, ALU.min, ALU.max)],
            [lambda v: v.tensor_scalar(kkc(3), kkc(3), B3_HI,
                                       B3_LO, ALU.min, ALU.max)]])
        wait("scalar", sVE, mk)
        SEt = se_grp([
            lambda s: s.activation(kkc(1), kkc(1), AF.Exp),
            lambda s: s.activation(kkc(2), kkc(2), AF.Sin),
            lambda s: s.activation(kkc(3), kkc(3), AF.Sin,
                                   bias=consts[:, 2:3])])
        wait("vector", sSE, SEt)
        ve_seq([
            [lambda v, n=ng: v.tensor_tensor(kkc(6), kkc(1),
                                             bcast_n(wre_sb, n), ALU.mult),
             lambda v, n=ng: v.tensor_tensor(kkc(7), kkc(1),
                                             bcast_n(wim_sb, n), ALU.mult)],
            [lambda v: v.tensor_tensor(kkc(4), kkc(6), kkc(3), ALU.mult),
             lambda v: v.tensor_tensor(kkc(9), kkc(7), kkc(2), ALU.mult),
             lambda v: v.tensor_tensor(kkc(5), kkc(6), kkc(2), ALU.mult),
             lambda v: v.tensor_tensor(kkc(8), kkc(7), kkc(3), ALU.mult)],
            [lambda v: v.tensor_tensor(kkc(4), kkc(4), kkc(9), ALU.subtract),
             lambda v: v.tensor_tensor(kkc(5), kkc(5), kkc(8), ALU.add)]])
        for c in range(8):
            if c > 0:
                ve_seq([
                    [lambda v, n=ng: v.tensor_tensor(
                        kkc(6), kkc(4), bcast_n(qc_sb, n), ALU.mult),
                     lambda v, n=ng: v.tensor_tensor(
                        kkc(7), kkc(5), bcast_n(qs_sb, n), ALU.mult),
                     lambda v, n=ng: v.tensor_tensor(
                        kkc(8), kkc(4), bcast_n(qs_sb, n), ALU.mult),
                     lambda v, n=ng: v.tensor_tensor(
                        kkc(9), kkc(5), bcast_n(qc_sb, n), ALU.mult)],
                    [lambda v: v.tensor_tensor(kkc(4), kkc(6),
                                               kkc(7), ALU.subtract),
                     lambda v: v.tensor_tensor(kkc(5), kkc(8),
                                               kkc(9), ALU.add)]])
            ve_seq([
                [lambda v: v.tensor_reduce(
                    kkc(6, 0, 64),
                    ap3(kkc(4, 0, 1), [(1, 64), (64, 8)]),
                    AX.X, ALU.add)],
                [lambda v, cc=c: v.tensor_tensor(
                    kacc[:, cc * 64:(cc + 1) * 64],
                    kacc[:, cc * 64:(cc + 1) * 64], kkc(6, 0, 64), ALU.add)]])
    VE_kb0 = ve_seq([
        [lambda v: v.tensor_tensor(
            kacc[:, 0:1], kacc[:, 0:1], dcol_sb[:, :], ALU.add),
         lambda v: v.memset(ybf[:, 0:768], 0.0)]])
    VE_kbuf = ve_seq([
        [lambda v: v.tensor_copy(ybf[:, 128:640], kacc[:, :])]])

    if debug:
        wait("sync", sVE, VE_kbuf)
        ld(dbg["kbuf"][:, :], ybf[:, 0:768], "db")
        ld(dbg["stats"][:, :], stfold[:, :], "db")

    if phase_limit < 2:
        quiesce()
        return _finish(block, Q, stack, nc)
    wait("sync", sVE, VE_kbuf)
    wait("sync", sVE, VE_m2r2)
    wait("gpsimd", sVE, VE_kbuf)
    VE_zu = ve_grp([lambda v: v.memset(u2np[:, :], 0.0)])

    LD_band, LD_u2, PE_conv, PE_tr = [], [], [], []
    VE_u2n, VE_lo, SE_hi, SE_yst, LD_yst, VE_yadd = [], [], [], [], [], []
    VE_bandc = []

    for h in range(min(HS, conv_h)):
        hb = h % 2
        if h >= 2 and conv_mode >= 2:
            wait("sync", sPE, PE_conv[h - 2])
        if h >= 2:
            wait("sync", sVE, VE_u2n[h - 2])
        if h >= 1:
            wait("gpsimd", sVE, VE_bandc[h - 1])
        bm = None
        for t in range(5):
            src = ap3(ybf[h:h + 1, 0:1], [(-1, 128), (1, 128)],
                      off=640 - 128 * t)
            bm = gld(outtm[1][:, t * 128:(t + 1) * 128], src, "bd")
        LD_band.append(bm)
        LD_u2.append(ld(u2raw[hb][:, :], x_conv[h, :, :, :], f"u2{hb}"))

        waitm("vector", LD_band[h])
        if h >= 2 and conv_mode >= 2:
            wait("vector", sPE, PE_conv[h - 2])
        mkb = ve_grp([lambda v, bb=hb: v.tensor_copy(
            band[bb][:, :], outtm[1][:, 0:640])])
        wait("vector", sVE, mkb)
        VE_bandc.append(ve_grp([lambda v, bb=hb: v.tensor_tensor(
            bandl[bb][:, :], outtm[1][:, 0:640], band[bb][:, :],
            ALU.subtract)]))
        waitm("vector", LD_u2[h])
        mku = ve_grp([lambda v, bb=hb: v.tensor_tensor(
            scrap[:, 0:256], u2raw[bb][:, :], r2_sb[:, :], ALU.mult)])
        wait("vector", sVE, mku)
        if h == 0:
            wait("vector", sVE, VE_zu)
        mkn = ve_grp([lambda v: v.tensor_tensor(
            ap3(u2np[0:128, 4:5], [(36, 8), (1, 32)]),
            scrap[:, 0:256], m2r2_sb[:, :], ALU.subtract)])
        wait("vector", sVE, mkn)
        mkh = ve_grp([lambda v, bb=hb: v.tensor_copy(
            u2n[bb][:, :], u2np[:, :])])
        wait("vector", sVE, mkh)
        VE_u2n.append(ve_grp([lambda v, bb=hb: v.tensor_tensor(
            u2l[bb][:, :], u2np[:, :], u2n[bb][:, :], ALU.subtract)]))

        if conv_mode < 2:
            continue
        wait("tensor", sVE, VE_bandc[h])
        wait("tensor", sVE, VE_u2n[h])
        if h >= 2:
            wait("tensor", sSE, SE_hi[h - 2])
            wait("tensor", sVE, VE_lo[h - 2])
        fns = []
        nmm = 0
        for t in range(5):
            for lhs_b, rhs_b in ((band, u2n), (band, u2l), (bandl, u2n)):
                rhs_t = ap3(rhs_b[hb][0:128, 0:1], [(36, 8), (1, 32)], off=t)
                fns.append(lambda x, bb=hb, tt=t, r=rhs_t, lb=lhs_b,
                           st=(nmm == 0), sp=(nmm == 14): x.matmul(
                    pf[bb][:, 0:256], lb[bb][:, tt * 128:(tt + 1) * 128], r,
                    start=st, stop=sp))
                nmm += 1
        PE_conv.append(pe_grp(fns))

        wait("scalar", sPE, PE_conv[h])
        if h >= 2 and conv_mode >= 3:
            wait("scalar", sPE, PE_tr[h - 2])
        SE_hi.append(se_grp([lambda s, bb=hb: s.activation(
            yhi[bb][:, :], pf[bb][:, 0:256], AF.Identity)]))
        wait("vector", sSE, SE_hi[h])
        if h >= 2 and conv_mode >= 3:
            wait("vector", sPE, PE_tr[h - 2])
        VE_lo.append(ve_grp([lambda v, bb=hb: v.tensor_tensor(
            ylo[bb][:, :], pf[bb][:, 0:256], yhi[bb][:, :], ALU.subtract)]))
        if conv_mode < 3:
            continue
        wait("tensor", sVE, VE_lo[h])
        if h >= 2:
            wait("tensor", sVE, VE_yadd[h - 2])
        fns = []
        for half in range(2):
            fns.append(lambda x, bb=hb, hf=half: x.matmul(
                pb[bb][:, hf * 256:hf * 256 + 128],
                yhi[bb][:, hf * 128:hf * 128 + 128], ident_b[:, :],
                start=True, stop=True, is_transpose=True))
            fns.append(lambda x, bb=hb, hf=half: x.matmul(
                pb[bb][:, hf * 256 + 128:hf * 256 + 256],
                ylo[bb][:, hf * 128:hf * 128 + 128], ident_b[:, :],
                start=True, stop=True, is_transpose=True))
        PE_tr.append(pe_grp(fns))
        wait("scalar", sPE, PE_tr[h])
        if h >= 1 and len(VE_yadd) >= h:
            wait("scalar", sVE, VE_yadd[h - 1])
        SE_yst.append(se_grp([lambda s, bb=hb: s.activation(
            scrap[:, 512:768],
            ap3(pb[bb][0:128, 0:1], [(256, 2), (1, 128)]), AF.Identity)]))
        wait("vector", sSE, SE_yst[h])
        if h >= 2 and len(LD_yst) > h - 2:
            waitm("vector", LD_yst[h - 2])
        VE_yadd.append(ve_grp([lambda v, bb=hb: v.tensor_tensor(
            ystage[bb][:, :], scrap[:, 512:768],
            ap3(pb[bb][0:128, 0:1], [(256, 2), (1, 128)], off=128),
            ALU.add)]))
        if conv_mode < 4:
            continue
        wait("sync", sVE, VE_yadd[h])
        am = None
        for half in range(2):
            dst = ap3(agin[0:1, 0:1],
                      [(4096 * 128, 4), (128, 32), (1, 128)],
                      off=(half * 4 * 4096 + h * 32) * 128)
            srcp = ystage[hb][:, half * 128:half * 128 + 128]
            am = ld(dst, srcp, f"ag{hb}")
        LD_yst.append(am)

    if phase_limit < 3:
        quiesce()
        return _finish(block, Q, stack, nc)
    for stn in ("ag0", "ag1"):
        if stn in DS:
            wait("gpsimd", DS[stn][0], DS[stn][1])

    def emit_a2a(g):
        with allow_ici_all_to_all():
            return g.collective_compute(
                "AllToAll", ALU.bypass, replica_groups=[list(range(NC))],
                ins=[agin[:, :].opt()], outs=[yall[:, :].opt()])
    emit("gpsimd", lambda g: emit_a2a(g).then_inc(sC2))

    out_rows = out_d[:, :].rearrange("(j p) d -> j p d", p=128)
    vscr_rows = vscr[:, :].rearrange("(j p) d -> j p d", p=128)
    M = {}
    waited_c2 = [False]
    SE_conv_last = SE_yst[-1] if SE_yst else 0
    VE_conv_last = VE_yadd[-1] if VE_yadd else 0

    if phase_limit < 4:
        wait("sync", sC2, 1)
        quiesce()
        return _finish(block, Q, stack, nc)
    for s in range(min(NT512, s_limit)):
        for jj in range(4):
            J = 4 * s + jj
            if J < 2:
                wait("sync", sSE, SE_stats)
            if ("h1tm", J - 2) in M:
                wait("sync", sVE, M[("h1tm", J - 2)])
            LDx = ld(x_tm[J % 2][:, :], x_rows[J, :, :], f"xa{J % 2}")
            waitm("vector", LDx)
            if ("h1tr", J - 2) in M:
                wait("vector", sPE, M[("h1tr", J - 2)])
            if s > 0 and jj == 0:
                wait("vector", sPE, M[("zmm", s - 1)])
            M[("h1tm", J)] = ve_grp([
                lambda v, jJ=J: v.tensor_scalar(
                    h1tm[jJ % 2][:, :], x_tm[jJ % 2][:, :],
                    st_s2[:, jJ:jJ + 1], st_nms2[:, jJ:jJ + 1],
                    ALU.mult, ALU.add)])
            wait("tensor", sVE, M[("h1tm", J)])
            if jj == 0 and s == 0:
                wait("tensor", sSE, SE_conv_last)
                wait("tensor", sVE, VE_conv_last)
            if jj > 0:
                wait("tensor", sVE, M[("h1fm", J - 1)])
            elif s > 0:
                wait("tensor", sVE, M[("h2fm", 4 * s - 1)])
            fns = [lambda x, jJ=J, c=ch: x.matmul(
                pb[0][:, c * 128:(c + 1) * 128],
                h1tm[jJ % 2][:, c * 128:(c + 1) * 128], ident_b[:, :],
                start=True, stop=True, is_transpose=True) for ch in range(8)]
            M[("h1tr", J)] = pe_grp(fns)
            wait("vector", sPE, M[("h1tr", J)])
            M[("h1fm", J)] = ve_grp([lambda v, j2=jj: v.tensor_copy(
                h1fm[:, j2 * 1024:(j2 + 1) * 1024], pb[0][:, :])])

        wait("tensor", sVE, M[("h1fm", 4 * s + 3)])
        if s > 0:
            wait("tensor", sVE, M[("weev", s - 1, 1, NE - 1)])
            wait("tensor", sVE, M[("fin", 4 * s - 1)])
        for oc in range(8):
            bank = oc % 2
            if oc >= 2:
                wait("tensor", sVE, M[("qev", s, oc - 2)])
            fns = [lambda x, o=oc, k=kc, bk=bank: x.matmul(
                pf[bk][:, :],
                wq_sb[:, k * 1024 + o * 128:k * 1024 + (o + 1) * 128],
                ap3(h1fm[0:128, 0:1], [(1024, 4), (1, 128)], off=k * 128),
                start=(k == 0), stop=(k == 7)) for kc in range(8)]
            M[("qmm", s, oc)] = pe_grp(fns)
            wait("vector", sPE, M[("qmm", s, oc)])
            M[("qev", s, oc)] = ve_grp([lambda v, o=oc, bk=bank: v.tensor_scalar(
                qf[:, o * 512:(o + 1) * 512], pf[bk][:, :],
                bq_sb[:, o:o + 1], None, ALU.add, ALU.bypass)])
        fns = [lambda x, k=kc: x.matmul(
            pf[2][0:64, :], wdown_sb[:, k * 64:(k + 1) * 64],
            ap3(h1fm[0:128, 0:1], [(1024, 4), (1, 128)], off=k * 128),
            start=(k == 0), stop=(k == 7)) for kc in range(8)]
        M[("zmm", s)] = pe_grp(fns)
        wait("vector", sPE, M[("zmm", s)])
        M[("zev", s)] = ve_grp([lambda v: v.tensor_scalar(
            zaug[0:64, :], pf[2][0:64, :], bdown_sb[:, 0:1], None,
            ALU.add, ALU.bypass)])
        wait("tensor", sVE, M[("zev", s)])
        for ch in range(8):
            bank = 3 + ch % 2
            if ch >= 2:
                wait("tensor", sVE, M[("kfev", s, ch - 2)])
            M[("kfmm", s, ch)] = pe_grp([lambda x, c=ch, bk=bank: x.matmul(
                pf[bk][:, :], wupk_sb[0:65, c * 128:(c + 1) * 128],
                zaug[0:65, :], start=True, stop=True)])
            wait("vector", sPE, M[("kfmm", s, ch)])
            M[("kfev", s, ch)] = ve_grp([lambda v, c=ch, bk=bank: v.tensor_copy(
                kfb[:, c * 512:(c + 1) * 512], pf[bk][:, :])])

        for jj in range(4):
            J = 4 * s + jj
            wait("tensor", sVE, M[("kfev", s, 7)])
            if ("xmid", J - 1) in M:
                wait("tensor", sVE, M[("xmid", J - 1)])
            if ("vev", J - 2) in M:
                wait("tensor", sVE, M[("vev", J - 2)])
            fns = [lambda x, j2=jj, hf=half: x.matmul(
                pf[4 + hf][:, :], zaug[0:65, j2 * 128:(j2 + 1) * 128],
                wupv_sb[0:65, hf * 512:hf * 512 + 512],
                start=True, stop=True) for half in range(2)]
            M[("vmm", J)] = pe_grp(fns)
            wait("vector", sPE, M[("vmm", J)])
            M[("vev", J)] = ve_grp([
                lambda v, jJ=J: v.tensor_copy(vtm[jJ % 2][:, 0:512],
                                              pf[4][:, :]),
                lambda v, jJ=J: v.tensor_copy(vtm[jJ % 2][:, 512:1024],
                                              pf[5][:, :])])
            wait("sync", sVE, M[("vev", J)])
            M[("vst", J)] = ld(vscr_rows[J, :, :], vtm[J % 2][:, :], "vs")
            waitm("sync", M[("vst", J)])
            if ("avmm", J - 1) in M:
                wait("sync", sPE, M[("avmm", J - 1)])
            srcv = ap3(vscr[0:1, 0:1],
                       [(1024, 8), (64, 16), (8192, 16), (1, 64)],
                       off=J * 128 * 1024)
            dstv = ap3(vstk[0:128, 0:1], [(66, 16), (1, 64)])
            M[("vstk", J)] = ld(dstv, srcv, "vk")

            if ("qkmm", J - 1) in M:
                wait("vector", sPE, M[("qkmm", J - 1)])
            if jj == 0 and ("yhi", 4 * s - 1) in M:
                wait("vector", sSE, M[("yhi", 4 * s - 1)])
            if J == 0 and "bd" in DS:
                wait("vector", DS["bd"][0], DS["bd"][1])
            fns = []
            for g2 in range(2):
                for hf in range(2):
                    dst = ap3(xnd[hf * 64:hf * 64 + 64, 0:1],
                              [(128, 8), (16, 8), (1, 8)], off=g2 * 8)
                    srcp = ap3(kfb[g2 * 64:g2 * 64 + 64, 0:1],
                               [(16, 8), (1, 8), (512, 8)],
                               off=jj * 128 + hf * 8)
                    fns.append(lambda v, d=dst, sp=srcp: v.tensor_copy(d, sp))
            for h2 in range(2):
                for hf in range(2):
                    dst = ap3(ybf[hf * 64:hf * 64 + 64, 0:1],
                              [(128, 8), (16, 8), (1, 8)], off=h2 * 8)
                    srcp = ap3(qf[h2 * 64:h2 * 64 + 64, 0:1],
                               [(16, 8), (1, 8), (512, 8)],
                               off=jj * 128 + hf * 8)
                    fns.append(lambda v, d=dst, sp=srcp: v.tensor_copy(d, sp))
            mkp = ve_grp(fns)
            wait("vector", sVE, mkp)
            M[("pack", J)] = ve_grp([
                lambda v: v.tensor_copy(kp_sb[:, :], xnd[:, :]),
                lambda v: v.tensor_copy(qp_sb[:, :], ybf[:, :])])

            wait("tensor", sVE, M[("pack", J)])
            if jj == 0:
                wait("tensor", sVE, M[("qev", s, 7)])
                wait("tensor", sVE, M[("zev", s)])
            if ("eexp", J - 1) in M:
                wait("tensor", sSE, M[("eexp", J - 1)])
            if ("rec", J - 1) in M:
                wait("tensor", sVE, M[("rec", J - 1)])
            fns = []
            for m in range(16):
                hf, q4 = m % 2, m // 2
                bank = 2 * hf + (q4 % 2)
                col = (q4 // 2) * 128
                fns.append(lambda x, mm=m, h_=hf, bk=bank, cl=col: x.matmul(
                    pf[bk][:, cl:cl + 128],
                    kp_sb[h_ * 64:h_ * 64 + 64,
                          (mm // 2) * 128:(mm // 2) * 128 + 128],
                    qp_sb[h_ * 64:h_ * 64 + 64,
                          (mm // 2) * 128:(mm // 2) * 128 + 128],
                    start=True, stop=True, tile_position=(64 * h_, 0)))
            M[("qkmm", J)] = pe_grp(fns)
            M[("packc", J)] = M[("pack", J)]

            wait("scalar", sPE, M[("qkmm", J)])
            if ("e2", J - 1) in M:
                wait("scalar", sVE, M[("e2", J - 1)])
            fns = [lambda s_, bk=bank: s_.activation(
                e_sb[:, bk * 512:(bk + 1) * 512], pf[bk][:, :], AF.Exp)
                for bank in range(4)]
            M[("eexp", J)] = se_grp(fns)
            wait("vector", sSE, M[("eexp", J)])
            maskrep = ap3(mask_b[0:128, 0:1], [(0, 16), (1, 128)])
            M[("e2", J)] = ve_grp([lambda v, mr=maskrep: v.tensor_tensor(
                e_sb[:, :], e_sb[:, :], mr, ALU.mult)])

            wait("tensor", sVE, M[("e2", J)])
            waitm("tensor", M[("vstk", J)])
            wait("tensor", sVE, M[("vev", J)])
            fns = []
            for m in range(16):
                hfm, q4m = m % 2, m // 2
                ec = (2 * hfm + (q4m % 2)) * 512 + (q4m // 2) * 128
                bank = 4 + m // 8
                col = (m % 8) * 64
                fns.append(lambda x, mm=m, bk=bank, cl=col, e0=ec: x.matmul(
                    pf[bk][:, cl:cl + 64], e_sb[:, e0:e0 + 128],
                    vstk[:, mm * 66:mm * 66 + 64], start=True, stop=True))
                fns.append(lambda x, mm=m, e0=ec: x.matmul(
                    pf[2][:, 384 + mm:385 + mm],
                    e_sb[:, e0:e0 + 128],
                    vstk[:, mm * 66 + 64:mm * 66 + 65],
                    start=True, stop=True))
            M[("avmm", J)] = pe_grp(fns)

            wait("vector", sPE, M[("avmm", J)])
            fns = [lambda v: v.reciprocal(recb[:, 0:16], pf[2][:, 384:400])]
            M[("rec", J)] = ve_grp(fns)
            wait("vector", sVE, M[("rec", J)])
            fns = []
            for m in range(16):
                bank = 4 + m // 8
                col = (m % 8) * 64
                fns.append(lambda v, mm=m, bk=bank, cl=col: v.tensor_scalar(
                    avn[:, mm * 64:(mm + 1) * 64], pf[bk][:, cl:cl + 64],
                    recb[:, mm:mm + 1], None, ALU.mult, ALU.bypass))
            M[("avn", J)] = ve_grp(fns)

            wait("tensor", sVE, M[("avn", J)])
            wait("tensor", sVE, M[("h1fm", 4 * s + 3)])
            if jj > 0:
                wait("tensor", sVE, M[("h2fm", J - 1)])
            fns = [lambda x, mm=m: x.matmul(
                pb[mm // 8][0:64, (mm % 8) * 128:(mm % 8) * 128 + 128],
                avn[:, mm * 64:(mm + 1) * 64], ident_b[:, :],
                start=True, stop=True, is_transpose=True) for m in range(16)]
            M[("avtr", J)] = pe_grp(fns)
            wait("vector", sPE, M[("avtr", J)])
            fns = []
            for par in range(2):
                for bk in range(2):
                    dstp = ap3(xnd[par * 64:par * 64 + 64, 0:1],
                               [(128, 8), (8, 8), (1, 8)], off=bk * 64)
                    srcp = ap3(pb[bk][0:64, 0:1],
                               [(1, 8), (128, 8), (16, 8)], off=par * 8)
                    fns.append(lambda v, d=dstp, sp=srcp: v.tensor_copy(d, sp))
            mda = ve_grp(fns)
            wait("vector", sVE, mda)
            M[("aof", J)] = ve_grp([lambda v, j2=jj: v.tensor_copy(
                aof[:, j2 * 1024:(j2 + 1) * 1024], xnd[:, :])])

            if ("acc", J - 2) in M:
                wait("sync", sVE, M[("acc", J - 2)])
            M[("ldxd", J)] = ld(xd[J % 2][:, :], x_rows[J, :, :],
                                f"xd{J % 2}")
            wait("tensor", sVE, M[("aof", J)])
            wait("tensor", sVE, M[("avn", J)])
            fns = []
            for ch in range(8):
                for half in range(2):
                    fns.append(lambda x, c=ch, hf=half, j2=jj: x.matmul(
                        pf[4 + hf][:, :],
                        aof[:, j2 * 1024 + c * 128:j2 * 1024 + (c + 1) * 128],
                        wo_sb[:, c * 1024 + hf * 512:c * 1024 + (hf + 1) * 512],
                        start=(c == 0), stop=False))
            for half in range(2):
                fns.append(lambda x, hf=half: x.matmul(
                    pf[4 + hf][:, :], ones_r[0:1, :],
                    bo_sb[0:1, hf * 512:(hf + 1) * 512],
                    start=False, stop=True))
            M[("womm", J)] = pe_grp(fns)
            wait("vector", sPE, M[("womm", J)])
            waitm("vector", M[("ldxd", J)])
            mxn = ve_grp([lambda v, jJ=J: v.tensor_scalar(
                xnd[:, :], xd[jJ % 2][:, :], st_r[:, jJ:jJ + 1],
                st_nmr[:, jJ:jJ + 1], ALU.mult, ALU.add)])
            wait("vector", sVE, mxn)
            M[("xmid", J)] = ve_grp([lambda v, jJ=J, hf=half: v.tensor_tensor(
                xmid[jJ % 2][:, hf * 512:(hf + 1) * 512],
                xnd[:, hf * 512:(hf + 1) * 512],
                pf[4 + hf][:, :], ALU.add) for half in range(2)])
            wait("vector", sVE, M[("xmid", J)])
            M[("acc", J)] = ve_grp([lambda v, jJ=J, j2=jj: v.tensor_tensor(
                accb[:, j2 * 1024:(j2 + 1) * 1024], xd[jJ % 2][:, :],
                xmid[jJ % 2][:, :], ALU.add)])
            wait("scalar", sVE, M[("xmid", J)])
            M[("h2sq", J)] = se_grp([lambda s_, jJ=J: s_.activation(
                scrap[:, :], xmid[jJ % 2][:, :], AF.Square,
                accum_out=st_r3[:, jJ:jJ + 1])])
            wait("vector", sSE, M[("h2sq", J)])
            M[("h2v", J)] = ve_grp([lambda v, jJ=J: v.tensor_scalar(
                st_r3[:, jJ:jJ + 1], st_r3[:, jJ:jJ + 1], 1.0 / D, None,
                ALU.mult, ALU.bypass)])
            wait("scalar", sVE, M[("h2v", J)])
            mh = se_grp([lambda s_, jJ=J: s_.activation(
                st_r3[:, jJ:jJ + 1], st_r3[:, jJ:jJ + 1], AF.Ln,
                bias=consts[:, 1:2])])
            wait("scalar", sSE, mh)
            M[("h2r", J)] = se_grp([lambda s_, jJ=J: s_.activation(
                st_r3[:, jJ:jJ + 1], st_r3[:, jJ:jJ + 1], AF.Exp,
                scale=-0.5)])
            wait("vector", sSE, M[("h2r", J)])
            if ("h2tr", J - 1) in M:
                wait("vector", sPE, M[("h2tr", J - 1)])
            elif J == 0:
                waitm("vector", LD_westg_last)
            M[("h2tm", J)] = ve_grp([lambda v, jJ=J: v.tensor_scalar(
                h2tmb[:, :], xmid[jJ % 2][:, :], st_r3[:, jJ:jJ + 1],
                None, ALU.mult, ALU.bypass)])
            wait("tensor", sVE, M[("h2tm", J)])
            wait("tensor", sVE, M[("aof", J)])
            fns = [lambda x, c=ch: x.matmul(
                pb[0][:, c * 128:(c + 1) * 128],
                h2tmb[:, c * 128:(c + 1) * 128], ident_b[:, :],
                start=True, stop=True, is_transpose=True) for ch in range(8)]
            M[("h2tr", J)] = pe_grp(fns)
            wait("vector", sPE, M[("h2tr", J)])
            if s > 0 and jj == 0:
                wait("vector", sPE, M[("wemm", s - 1, 1, NE - 1, 7)])
            M[("h2fm", J)] = ve_grp([lambda v, j2=jj: v.tensor_copy(
                h2fm[:, j2 * 1024:(j2 + 1) * 1024], pb[0][:, :])])

        wait("tensor", sVE, M[("h2fm", 4 * s + 3)])
        wait("tensor", sVE, M[("rec", 4 * s + 3)])
        fns = [lambda x, k=kc: x.matmul(
            pf[2][0:8, :], wg_sb[:, k * 8:(k + 1) * 8],
            ap3(h2fm[0:128, 0:1], [(1024, 4), (1, 128)], off=k * 128),
            start=(k == 0), stop=(k == 7)) for kc in range(8)]
        M[("gmm", s)] = pe_grp(fns)
        wait("vector", sPE, M[("gmm", s)])
        M[("gev", s)] = ve_grp([lambda v: v.tensor_scalar(
            glog[0:8, :], pf[2][0:8, :], bg_sb[:, 0:1], None,
            ALU.add, ALU.bypass)])
        wait("tensor", sVE, M[("gev", s)])
        wait("tensor", sSE, M[("eexp", 4 * s + 3)])
        fns = [lambda x, j2=jj: x.matmul(
            pf[3][:, j2 * 8:(j2 + 1) * 8],
            glog[0:8, j2 * 128:(j2 + 1) * 128], ident_f[0:8, 0:8],
            start=True, stop=True, is_transpose=True) for jj in range(4)]
        M[("gtr", s)] = pe_grp(fns)
        wait("vector", sPE, M[("gtr", s)])
        M[("gtv", s)] = ve_grp([lambda v: v.tensor_copy(
            glogt[:, 0:32], pf[3][:, 0:32])])
        wait("scalar", sVE, M[("gtv", s)])
        if s > 0:
            wait("scalar", sVE, M[("weev", s - 1, 1, NE - 1)])
        M[("gexp", s)] = se_grp([lambda s_: s_.activation(
            gtm[:, 0:32], glogt[:, 0:32], AF.Exp)])
        wait("vector", sSE, M[("gexp", s)])
        mkg = ve_grp([lambda v, j2=jj: v.tensor_reduce(
            grec[:, j2:j2 + 1], gtm[:, j2 * 8:(j2 + 1) * 8], AX.X, ALU.add)
            for jj in range(4)])
        wait("vector", sVE, mkg)
        mkg2 = ve_grp([lambda v: v.reciprocal(grec[:, 4:8], grec[:, 0:4])])
        wait("vector", sVE, mkg2)
        M[("gate", s)] = ve_grp([lambda v, j2=jj: v.tensor_scalar(
            gtm[:, j2 * 8:(j2 + 1) * 8], gtm[:, j2 * 8:(j2 + 1) * 8],
            grec[:, 4 + j2:5 + j2], None, ALU.mult, ALU.bypass)
            for jj in range(4)])

        wchunk_marks = []
        kglob = [0]

        def we_slot_ap(k, hf):
            return we_sb[:, (k % 2) * 1024 + hf * 512:
                         (k % 2) * 1024 + (hf + 1) * 512]

        for pr in range(2):
            for eix in range(NE):
                for ch in range(8):
                    k = kglob[0]
                    if k >= 2:
                        wait("sync", sPE, wchunk_marks[k - 2])
                    M[("wechld", s, k)] = ld(
                        we_sb[:, (k % 2) * 1024:(k % 2) * 1024 + 1024],
                        webf_rows[eix * 8 + ch, :, :], f"we{k % 2}")
                    waitm("tensor", M[("wechld", s, k)])
                    if ch == 0:
                        if eix == 0 and pr == 0:
                            wait("tensor", sVE, M[("gate", s)])
                            wait("tensor", sVE, M[("gev", s)])
                            wait("tensor", sVE, M[("gtv", s)])
                            wait("tensor", sVE, M[("rec", 4 * s + 3)])
                            wait("tensor", sVE, M[("xmid", 4 * s + 3)])
                            for j4 in range(4):
                                wait("tensor", sVE, M[("acc", 4 * s + j4)])
                        elif eix == 0:
                            wait("tensor", sVE, M[("weev", s, 0, NE - 1)])
                        else:
                            wait("tensor", sVE, M[("weev", s, pr, eix - 1)])
                    fns = []
                    for jj2 in range(2):
                        jjx = 2 * pr + jj2
                        for hf in range(2):
                            fns.append(
                                lambda x, c=ch, j2=jjx, q2=jj2, hf_=hf,
                                kk=k: x.matmul(
                                    pf[2 + 2 * q2 + hf_][:, :],
                                    h2fm[:, j2 * 1024 + c * 128:
                                         j2 * 1024 + (c + 1) * 128],
                                    we_slot_ap(kk, hf_),
                                    start=(c == 0), stop=(c == 7)))
                    mk = pe_grp(fns)
                    wchunk_marks.append(mk)
                    M[("wemm", s, pr, eix, ch)] = mk
                    kglob[0] += 1
                wait("vector", sPE, M[("wemm", s, pr, eix, 7)])
                if eix > 0:
                    wait("vector", sVE, M[("weev", s, pr, eix - 1)])
                elif pr > 0:
                    wait("vector", sVE, M[("weev", s, 0, NE - 1)])
                fns = []
                for jj2 in range(2):
                    jjx = 2 * pr + jj2
                    for hf in range(2):
                        fns.append(
                            lambda v, e_=eix, j2=jjx, q2=jj2, hf_=hf:
                            v.scalar_tensor_tensor(
                                accb[:, j2 * 1024 + hf_ * 512:
                                     j2 * 1024 + (hf_ + 1) * 512],
                                pf[2 + 2 * q2 + hf_][:, :],
                                gtm[:, j2 * 8 + e_:j2 * 8 + e_ + 1],
                                accb[:, j2 * 1024 + hf_ * 512:
                                     j2 * 1024 + (hf_ + 1) * 512],
                                ALU.mult, ALU.add))
                M[("weev", s, pr, eix)] = ve_grp(fns)

        if not waited_c2[0]:
            wait("sync", sC2, 1)
            waited_c2[0] = True
        for jj in range(4):
            J = 4 * s + jj
            if ("ytr", J - 1) in M:
                wait("sync", sPE, M[("ytr", J - 1)])
            if jj == 0:
                wait("sync", sVE, M[("pack", 4 * s + 3)])
            ym = None
            for r in range(NC):
                srcy = ap3(yall[0:1, 0:1], [(32 * 128, 128), (1, 128)],
                           off=(r * 4096 + J) * 128)
                ym = ld(ybf[:, r * 128:(r + 1) * 128], srcy, "yl")
            M[("yld", J)] = ym
            waitm("scalar", M[("yld", J)])
            if ("ylo", J - 1) in M:
                wait("scalar", sVE, M[("ylo", J - 1)])
            M[("yhi", J)] = se_grp([lambda s_: s_.activation(
                ybhi[:, :], ybf[:, :], AF.Identity)])
            wait("vector", sSE, M[("yhi", J)])
            if ("ytr", J - 1) in M:
                wait("vector", sPE, M[("ytr", J - 1)])
            M[("ylo", J)] = ve_grp([lambda v: v.tensor_tensor(
                yblo[:, :], ybf[:, :], ybhi[:, :], ALU.subtract)])
            wait("tensor", sVE, M[("ylo", J)])
            wait("tensor", sVE, M[("weev", s, jj // 2, NE - 1)])
            if jj == 0:
                wait("tensor", sSE, M[("eexp", 4 * s + 3)])
                wait("tensor", sVE, M[("avn", 4 * s + 3)])
            if ("fin", J - 1) in M:
                wait("tensor", sVE, M[("fin", J - 1)])
            fns = []
            for r in range(NC):
                bk = r // 4
                cl = (r % 4) * 256
                fns.append(lambda x, rr=r, b_=bk, c_=cl: x.matmul(
                    pb[b_][:, c_:c_ + 128], ybhi[:, rr * 128:(rr + 1) * 128],
                    ident_b[:, :], start=True, stop=True, is_transpose=True))
                fns.append(lambda x, rr=r, b_=bk, c_=cl: x.matmul(
                    pb[b_][:, c_ + 128:c_ + 256],
                    yblo[:, rr * 128:(rr + 1) * 128],
                    ident_b[:, :], start=True, stop=True, is_transpose=True))
            M[("ytr", J)] = pe_grp(fns)
            wait("scalar", sPE, M[("ytr", J)])
            M[("yev", J)] = se_grp([lambda s_, bk=bk2: s_.activation(
                scrap[:, bk * 512:(bk + 1) * 512],
                ap3(pb[bk][0:128, 0:1], [(256, 4), (1, 128)]), AF.Identity)
                for bk2 in range(2)])
            wait("vector", sSE, M[("yev", J)])
            if ("outst", J - 2) in M:
                waitm("vector", M[("outst", J - 2)])
            mf1 = ve_grp(
                [lambda v, jJ=J, hf=hf2, j2=jj: v.tensor_tensor(
                    outtm[jJ % 2][:, hf * 512:(hf + 1) * 512],
                    accb[:, j2 * 1024 + hf * 512:j2 * 1024 + (hf + 1) * 512],
                    scrap[:, hf * 512:(hf + 1) * 512], ALU.add)
                 for hf2 in range(2)])
            wait("vector", sVE, mf1)
            M[("fin", J)] = ve_grp(
                [lambda v, jJ=J, hf=hf2: v.tensor_tensor(
                    outtm[jJ % 2][:, hf * 512:(hf + 1) * 512],
                    outtm[jJ % 2][:, hf * 512:(hf + 1) * 512],
                    ap3(pb[hf][0:128, 0:1], [(256, 4), (1, 128)], off=128),
                    ALU.add) for hf2 in range(2)])
            wait("sync", sVE, M[("fin", J)])
            M[("outst", J)] = ld(out_rows[J, :, :], outtm[J % 2][:, :],
                                 f"os{J % 2}")

        if debug and s == 0:
            wait("sync", sVE, M[("fin", 3)])
            wait("gpsimd", sVE, M[("fin", 3)])
            for dnm, srcb in [("qf", qf), ("kf", kfb), ("aof", aof),
                              ("h2", h2fm), ("e2", e_sb), ("vstk", vstk),
                              ("u2n", u2n[1])]:
                gld(dbg[dnm][:, :], srcb[:, :], "db")
            ld(dbg["acc"][:, :], accb[:, :], "db")
            ld(dbg["xmid"][:, :], xmid[1][:, :], "db")
            ld(dbg["xnd"][:, :], xnd[:, :], "db")
            ld(dbg["gt"][:, :], gtm[:, :], "db")
            ld(dbg["ys"][:, :], ybf[:, :], "db")
            for e_ in ("vector", "scalar", "tensor", "gpsimd"):
                wait(e_, DS["db"][0], DS["db"][1])

    quiesce()
    return _finish(block, Q, stack, nc)


def _finish(block, Q, stack, nc):
    @block.sync
    def _(sync):
        for f in Q["sync"]:
            f(sync)

    @block.tensor
    def _(tensor):
        for f in Q["tensor"]:
            f(tensor)

    @block.vector
    def _(vector):
        for f in Q["vector"]:
            f(vector)

    @block.scalar
    def _(scalar):
        for f in Q["scalar"]:
            f(scalar)

    @block.gpsimd
    def _(gpsimd):
        for f in Q["gpsimd"]:
            f(gpsimd)

    stack.close()
    return nc


def kernel(L0_attn_u, L0_attn_D, L0_attn_Lambda_re, L0_attn_Lambda_im,
           L0_attn_log_dt, L0_attn_W_re, L0_attn_W_im, L0_ffn_gamma1,
           L0_ffn_Wq, L0_ffn_bq, L0_ffn_Wdown, L0_ffn_bdown, L0_ffn_Wup,
           L0_ffn_bup, L0_ffn_Wo, L0_ffn_bo, L0_ffn_gamma2, L0_ffn_Wg,
           L0_ffn_bg, L0_ffn_We):
    os.environ["TRN_ALLOW_ICI_ALL_TO_ALL"] = "1"
    u = np.ascontiguousarray(np.asarray(L0_attn_u, dtype=np.float32))
    cont = np.ascontiguousarray

    def fold_feat(vec, p=128):
        v = np.asarray(vec, dtype=np.float32).reshape(-1, p)
        return cont(v.T)

    nc = build_program(debug=False)
    in_maps = []
    for c in range(NC):
        hs = slice(128 * c, 128 * (c + 1))
        v = u[:, :, hs]
        w = v.reshape(B, 32, 128, 128)
        x_conv = cont(w.transpose(3, 2, 0, 1))
        im = {
            "x_tok": cont(u[c]),
            "x_conv": x_conv,
            "wq": cont(np.asarray(L0_ffn_Wq, np.float32)),
            "bq_f": fold_feat(L0_ffn_bq),
            "wdown": cont(np.asarray(L0_ffn_Wdown, np.float32)),
            "bdown_f": cont(np.asarray(L0_ffn_bdown, np.float32).reshape(64, 1)),
            "wup": cont(np.asarray(L0_ffn_Wup, np.float32)),
            "bup_r": cont(np.asarray(L0_ffn_bup, np.float32).reshape(1, 2048)),
            "wo": cont(np.asarray(L0_ffn_Wo, np.float32)),
            "bo_r": cont(np.asarray(L0_ffn_bo, np.float32).reshape(1, 1024)),
            "wg": cont(np.asarray(L0_ffn_Wg, np.float32)),
            "bg_f": cont(np.asarray(L0_ffn_bg, np.float32).reshape(8, 1)),
            "we": cont(np.asarray(L0_ffn_We, np.float32)),
            "g1_f": fold_feat(L0_ffn_gamma1),
            "g2_f": fold_feat(L0_ffn_gamma2),
            "logdt_r": cont(np.asarray(L0_attn_log_dt, np.float32)[hs]
                            .reshape(1, 128)),
            "lre_r": cont(np.asarray(L0_attn_Lambda_re, np.float32)
                          .reshape(1, 64)),
            "lim_r": cont(np.asarray(L0_attn_Lambda_im, np.float32)
                          .reshape(1, 64)),
            "wre_h": cont(np.asarray(L0_attn_W_re, np.float32)[hs]),
            "wim_h": cont(np.asarray(L0_attn_W_im, np.float32)[hs]),
            "dcol_h": cont(np.asarray(L0_attn_D, np.float32)[hs]
                           .reshape(128, 1)),
        }
        in_maps.append(im)

    res = run_bass_kernel_spmd(nc, in_maps, core_ids=list(range(NC)))
    out = np.stack([np.asarray(res.results[c]["out"]) for c in range(NC)],
                   axis=0)
    return out
